# version 01 (final)
# baseline (speedup 1.0000x reference)
import os
import numpy as np
import ml_dtypes
from contextlib import ExitStack
import concourse.bass as bass
import concourse.mybir as mybir
from concourse.bass_utils import run_bass_kernel_spmd

F32 = mybir.dt.float32
BF16 = mybir.dt.bfloat16
U32 = mybir.dt.uint32
AF = mybir.ActivationFunctionType
ALU = mybir.AluOpType
AX = mybir.AxisListType

P = 128
T = 2048
D = 1024
DC = D // P
FF = 4096
FC = FF // P
H = 16
NE = 8
CAP = 2048
NSLOT = NE * CAP
NT = T // P


class Sem:
    def __init__(self, nc, es, name):
        self.h = es.enter_context(nc.semaphore(name))
        self.n = 0

    def inc(self, instr, k=1):
        instr.then_inc(self.h, k)
        self.n += k


class Arena:
    def __init__(self, nc, es, name, nbytes):
        self.t = es.enter_context(nc.sbuf_tensor(name, [P, nbytes], mybir.dt.uint8))
        self.nbytes = nbytes
        self.off = 0
        self.stack = []
        self.peak = 0

    def alloc(self, shape, dtype):
        assert shape[0] == P
        esz = mybir.dt.size(dtype)
        free = int(np.prod(shape[1:]))
        nb = free * esz
        off = (self.off + 63) & ~63
        assert off + nb <= self.nbytes, f"arena overflow {off}+{nb}>{self.nbytes}"
        self.off = off + nb
        self.peak = max(self.peak, self.off)
        ap = self.t[:, off:off + nb].bitcast(dtype)
        if len(shape) > 2:
            letters = "abcdefg"[: len(shape) - 1]
            spec = f"p ({' '.join(letters)}) -> p {' '.join(letters)}"
            ap = ap.rearrange(spec, **{letters[i]: shape[1 + i] for i in range(len(shape) - 1)})
        return ap

    def push(self):
        self.stack.append(self.off)

    def pop(self):
        self.off = self.stack.pop()

    def seek(self, off):
        self.off = off


def build_nc(upto=4):
    if upto > 11:
        upto = 4
    nc = bass.Bass("TRN2", target_bir_lowering=False)
    es = ExitStack()

    def din(name, shape, dtype=F32):
        return nc.dram_tensor(name, shape, dtype, kind="ExternalInput").ap()

    x_in = din("x", [D, T])
    cmask8 = din("cmask8", [NE, NE])
    wq = din("wq", [D, D]); wk = din("wk", [D, D]); wt = din("wt", [D, D])
    wqa = din("wqa", [D, P]); wka = din("wka", [D, P])
    bq_i = din("bq", [P, DC]); bk_i = din("bk", [P, DC]); bt_i = din("bt", [P, DC])
    bqa_i = din("bqa", [H, 2])
    w1t = din("w1t", [D, FF]); w2t = din("w2t", [D, FF])
    h1t = din("h1t", [FF, FF]); h2t = din("h2t", [FF, FF])
    w3t = din("w3t", [FF, D])
    s1_i = din("s1c", [P, FC]); s2_i = din("s2c", [P, FC])
    sh1_i = din("sh1c", [P, FC]); sh2_i = din("sh2c", [P, FC])
    s3_i = din("s3c", [P, DC])
    wint = din("wint", [D, D]); wgt = din("wgt", [D, D])
    bin_i = din("binc", [P, DC]); bg_i = din("bgc", [P, DC])
    lam_i = din("lamc", [P, DC])
    wgate = din("wgate", [D, P])
    cfc = din("cfc", [NE, D, 3072])
    cprj = din("cprj", [NE, 1536, D])
    eyeb_i = din("eyebf", [P, P], BF16)
    eye8_i = din("eye8f", [8, 8])
    sel16_i = din("sel16", [H, DC * P], BF16)
    onescol_i = din("onescol", [P, 1], BF16)
    onesfull_i = din("onesfull", [P, P], BF16)
    ones1p_i = din("ones1p", [1, P])
    iota8_i = din("iota8", [P, NT * NE])
    cpool_i = din("cpool", [P, 8])

    y_out = nc.dram_tensor("y", [D, T], F32, kind="ExternalOutput").ap()
    xspill = nc.dram_tensor("xspill", [D, T], F32).ap()
    exp_in = nc.dram_tensor("exp_in", [NSLOT, D], BF16).ap()
    out_slots = nc.dram_tensor("out_slots", [NSLOT, D], BF16).ap()
    ccin = nc.dram_tensor("ccin", [NE], F32).ap()
    ccout = nc.dram_tensor("ccout", [NE * NE], F32).ap()

    AR = Arena(nc, es, "arena", 206 * 1024)
    CONST_OFF = 196 * 1024

    x = AR.alloc([P, DC, T], F32)
    AR.seek(CONST_OFF)
    bq = AR.alloc([P, DC], F32); bk = AR.alloc([P, DC], F32); bt = AR.alloc([P, DC], F32)
    binc = AR.alloc([P, DC], F32); bgc = AR.alloc([P, DC], F32)
    clam = AR.alloc([P, DC], F32)
    cpool = AR.alloc([P, 8], F32)
    s1v = AR.alloc([P, FC], F32); s2v = AR.alloc([P, FC], F32)
    sh1v = AR.alloc([P, FC], F32); sh2v = AR.alloc([P, FC], F32)
    s3v = AR.alloc([P, DC], F32)
    bqa2 = AR.alloc([P, 2], F32)
    eyeb = AR.alloc([P, P], BF16)
    eye8 = AR.alloc([P, 8], F32)
    sel16 = AR.alloc([P, DC * P], BF16)
    onescol = AR.alloc([P, 1], BF16)
    onesfull = AR.alloc([P, P], BF16)
    ones1p = AR.alloc([P, P], F32)
    iota8 = AR.alloc([P, NT, NE], F32)
    cm8 = AR.alloc([P, NE], F32)
    wgate_s = AR.alloc([P, DC, P], BF16)
    assert AR.off <= 206 * 1024, AR.off

    pA = es.enter_context(nc.psum_tensor("pA", [P, T], F32))
    pB = es.enter_context(nc.psum_tensor("pB", [P, T], F32))
    pAb = pA.ap().bitcast(BF16)
    pBb = pB.ap().bitcast(BF16)

    sT = Sem(nc, es, "sT"); sV = Sem(nc, es, "sV"); sS = Sem(nc, es, "sS")
    sG = Sem(nc, es, "sG")
    dW = Sem(nc, es, "dW"); dX = Sem(nc, es, "dX"); dS = Sem(nc, es, "dS")
    dI = Sem(nc, es, "dI"); sC = Sem(nc, es, "sC")

    block = es.enter_context(nc.Block())

    class EngTrack:
        def __init__(self, eng):
            object.__setattr__(self, "_e", eng)
            object.__setattr__(self, "last", None)

        def __getattr__(self, name):
            f = getattr(self._e, name)
            if callable(f) and name not in ("wait_ge", "nop"):
                def w(*a, **k):
                    r = f(*a, **k)
                    if r is not None and hasattr(r, "then_inc"):
                        object.__setattr__(self, "last", r)
                    return r
                return w
            return f

        def mark(self, sem, k=1):
            if hasattr(self._e, "engine_nop") and self.last is not None:
                tgt = self._e.engine_nop()
            else:
                tgt = self.last if self.last is not None else self._e.nop()
            sem.inc(tgt, k)

    def T_(fn): block.tensor(lambda e: fn(EngTrack(e)))
    def V_(fn): block.vector(lambda e: fn(EngTrack(e)))
    def S_(fn): block.scalar(lambda e: fn(EngTrack(e)))
    def G_(fn): block.gpsimd(lambda e: fn(EngTrack(e)))
    def SY(fn): block.sync(lambda e: fn(EngTrack(e)))

    def prologue():
        def f(sy):
            dX.inc(sy.dma_start(out=x[:, :, :], in_=x_in.rearrange("(c p) t -> p c t", p=P)), 16)
            for dst, src in [(bq, bq_i), (bk, bk_i), (bt, bt_i), (binc, bin_i),
                             (bgc, bg_i), (clam, lam_i), (cpool, cpool_i),
                             (s1v, s1_i), (s2v, s2_i), (sh1v, sh1_i), (sh2v, sh2_i),
                             (s3v, s3_i), (eyeb, eyeb_i), (onescol, onescol_i),
                             (onesfull, onesfull_i),
                             (iota8.rearrange("p a b -> p (a b)"), iota8_i)]:
                dX.inc(sy.dma_start(out=dst, in_=src), 16)
            dX.inc(sy.dma_start(out=bqa2[0:H, :], in_=bqa_i), 16)
            dX.inc(sy.dma_start(out=eye8[0:8, :], in_=eye8_i), 16)
            dX.inc(sy.dma_start(out=sel16[0:H, :], in_=sel16_i), 16)
            dX.inc(sy.dma_start(out=ones1p[0:1, :], in_=ones1p_i), 16)
            dX.inc(sy.dma_start(out=cm8[0:8, :], in_=cmask8), 16)
        SY(f)
        ld_mark = dX.n

        def g(g_):
            dW.inc(g_.dma_start(out=wgate_s[:, :, :],
                                in_=wgate.rearrange("(c p) e -> p c e", p=P)), 16)
        G_(g)

        def s(s_):
            s_.wait_ge(dX.h, ld_mark)
            s_.activation(clam[:, :], clam[:, :], AF.Exp, bias=0.0, scale=1.0)
            s_.nop(); s_.nop()
            s_.activation(clam[:, :], clam[:, :], AF.Ln, bias=cpool[:, 0:1], scale=1.0)
            s_.nop(); s_.nop()
            sS.inc(s_.mul(clam[:, :], clam[:, :], -4.0))
        S_(s)

    def emit_ln(t0, tn, xln_bf, xbfs, scr_a, scr_b, xsrc=None, extra_wait=None):
        ntile = tn // 512
        if xsrc is None:
            xsrc = x[:, :, t0:t0 + tn]

        def v1(v):
            v.wait_ge(dX.h, dX.n)
            if extra_wait is not None:
                v.wait_ge(extra_wait[0], extra_wait[1])
            for c in range(DC):
                v.tensor_copy(xbfs[:, c, :], xsrc[:, c, :])
            v.mark(sV)
        V_(v1)
        cast_done = sV.n

        def t1(t):
            t.wait_ge(sV.h, cast_done)
            for tt in range(ntile):
                for c in range(DC):
                    t.matmul(pA[:, tt * 512:(tt + 1) * 512], onesfull[:, :],
                             xbfs[:, c, tt * 512:(tt + 1) * 512],
                             start=(c == 0), stop=(c == DC - 1))
            t.mark(sT)
        T_(t1)
        s1_done = sT.n

        def s1(s):
            s.wait_ge(sT.h, s1_done)
            for c in range(DC):
                s.activation(xbfs[:, c, :], xbfs[:, c, :], AF.Square, bias=0.0, scale=1.0)
            s.mark(sS)
        S_(s1)
        sq_done = sS.n

        def t2(t):
            t.wait_ge(sS.h, sq_done)
            for tt in range(ntile):
                for c in range(DC):
                    t.matmul(pB[:, tt * 512:(tt + 1) * 512], onesfull[:, :],
                             xbfs[:, c, tt * 512:(tt + 1) * 512],
                             start=(c == 0), stop=(c == DC - 1))
            t.mark(sT)
        T_(t2)
        s2_done = sT.n

        def v2(v):
            v.wait_ge(sT.h, s2_done)
            v.tensor_scalar(scr_a[:, 0:tn], pA[:, 0:tn], 1.0 / D, None, ALU.mult)
            v.tensor_tensor(scr_b[:, 0:tn], scr_a[:, 0:tn], scr_a[:, 0:tn], ALU.mult)
            v.scalar_tensor_tensor(scr_b[:, 0:tn], pB[:, 0:tn], 1.0 / D,
                                   scr_b[:, 0:tn], ALU.mult, ALU.subtract)
            v.mark(sV)
        V_(v2)
        var_done = sV.n

        def s2(s):
            s.wait_ge(sV.h, var_done)
            s.activation(scr_b[:, 0:tn], scr_b[:, 0:tn], AF.Sqrt,
                         bias=cpool[:, 1:2], scale=1.0)
            s.mark(sS)
        S_(s2)
        sd_done = sS.n

        def v3(v):
            v.wait_ge(sS.h, sd_done)
            v.reciprocal(pA[:, 0:tn], scr_b[:, 0:tn])
            v.tensor_tensor(scr_a[:, 0:tn], scr_a[:, 0:tn], pA[:, 0:tn], ALU.mult)
            for c in range(DC):
                v.tensor_tensor(xbfs[:, c, :], xsrc[:, c, :], pA[:, 0:tn], ALU.mult)
            for c in range(DC):
                v.tensor_tensor(xln_bf[:, c, :], xbfs[:, c, :], scr_a[:, 0:tn], ALU.subtract)
            v.mark(sV)
        V_(v3)
        return sV.n

    def proj_d2d(wdram, xin_bf, wslab, nslab, out_epil, pre_wait):
        wsrc = wdram.rearrange("(c p) m -> p c m", p=P)
        wmark0 = dW.n
        free_marks = {}
        mm_marks = {}
        for oc in range(DC):
            b = oc % nslab

            def g(g_, oc=oc, b=b):
                if oc == 0:
                    g_.wait_ge(pre_wait[0], pre_wait[1])
                if oc >= nslab:
                    g_.wait_ge(sT.h, mm_marks[oc - nslab])
                dW.inc(g_.dma_start(out=wslab[b][:, 0:DC, :],
                                    in_=wsrc[:, :, oc * P:(oc + 1) * P]), 16)
            G_(g)

            def t(t_, oc=oc, b=b):
                if oc == 0:
                    t_.wait_ge(pre_wait[0], pre_wait[1])
                t_.wait_ge(dW.h, wmark0 + 16 * (oc + 1))
                if oc >= 2:
                    h_, v_ = free_marks[oc - 2]
                    t_.wait_ge(h_, v_)
                ps = pA if oc % 2 == 0 else pB
                for kc in range(DC):
                    for tt in range(4):
                        sl = slice(tt * 512, (tt + 1) * 512)
                        t_.matmul(ps[:, sl], wslab[b][:, kc, :], xin_bf[:, kc, sl],
                                  start=(kc == 0), stop=(kc == DC - 1))
                t_.mark(sT)
                mm_marks[oc] = sT.n
            T_(t)
            ps = pA if oc % 2 == 0 else pB
            free_marks[oc] = out_epil(oc, ps, lambda oc=oc: mm_marks[oc])
        return mm_marks

    def phase_l0_attn():
        DBG = os.environ.get("MK_DBG", "")
        AR.seek(64 * 1024)
        xln = AR.alloc([P, DC, T], BF16)
        mqb = AR.alloc([P, DC, T], BF16)
        mkb = AR.alloc([P, DC, T], BF16)
        scr = AR.alloc([P, T], F32)
        qsb = AR.alloc([P, T], F32)
        qwb = AR.alloc([P, T], BF16)
        pqk = AR.alloc([P, 2 * DC], F32)
        wsl = [AR.alloc([P, DC, P], BF16) for _ in range(2)]
        wasl = AR.alloc([P, DC, P], BF16)
        assert AR.off <= CONST_OFF, AR.off

        ln_done = emit_ln(0, T, xln, mqb, scr[:, 0:T], qsb[:, 0:T])

        def mk_epil(outb, bias):
            def epil(oc, ps, t_mark_fn):
                def s(s_, oc=oc, ps=ps):
                    s_.wait_ge(sT.h, t_mark_fn())
                    s_.activation(outb[:, oc, :], ps[:, :], AF.Identity,
                                  bias=bias[:, oc:oc + 1], scale=1.0)
                    s_.mark(sS)
                S_(s)
                return (sS.h, sS.n)
            return epil

        proj_d2d(wq, xln, wsl, 2, mk_epil(mqb, bq), (sV.h, ln_done))
        mq_done = sS.n

        def tapb(buf, wait_h, wait_n, rows=None):
            def vt(v):
                v.wait_ge(wait_h, wait_n)
                if rows is None:
                    for c in range(DC):
                        v.tensor_copy(x[:, c, :], buf[:, c, :])
                else:
                    for c in range(DC):
                        v.memset(x[:, c, :], 0.0)
                    v.tensor_copy(x[0:rows, 0, :], buf[0:rows, 0:T])
                v.mark(sV)
            V_(vt)
            return sV.n
        if DBG == "ln":
            return tapb(xln, sS.h, 0)
        if DBG == "mq":
            return tapb(mqb, sS.h, mq_done)
        proj_d2d(wk, xln, wsl, 2, mk_epil(mkb, bk), (sS.h, mq_done))
        mk_done = sS.n

        def score_pool(src_bf, wadram, bqacol, qscols, outcol0, pre_h, pre_n, dbg_qs0=False):
            wmark = dW.n

            def g(g_):
                g_.wait_ge(pre_h, pre_n)
                dW.inc(g_.dma_start(out=wasl[:, :, :],
                                    in_=wadram.rearrange("(c p) h -> p c h", p=P)), 16)
            G_(g)

            def t(t_):
                t_.wait_ge(dW.h, wmark + 16)
                t_.wait_ge(pre_h, pre_n)
                for tt in range(4):
                    sl = slice(tt * 512, (tt + 1) * 512)
                    for kc in range(DC):
                        t_.matmul(pA[:, sl], wasl[:, kc, :], src_bf[:, kc, sl],
                                  start=(kc == 0), stop=(kc == DC - 1))
                t_.mark(sT)
            T_(t)
            mmk = sT.n

            def s(s_):
                s_.wait_ge(sT.h, mmk)
                s_.activation(qsb[0:H, qscols], pA[0:H, 0:T], AF.Identity,
                              bias=bqacol, scale=0.125)
                s_.mark(sS)
            S_(s)
            sck = sS.n
            if dbg_qs0:
                return sck

            def v1(v):
                v.wait_ge(sS.h, sck)
                v.tensor_reduce(scr[0:H, 0:1], qsb[0:H, qscols], AX.X, ALU.max,
                                negate=True)
                v.mark(sV)
            V_(v1)
            vm = sV.n

            def s2(s_):
                s_.wait_ge(sV.h, vm)
                s_.activation(qsb[0:H, qscols], qsb[0:H, qscols], AF.Exp,
                              bias=scr[0:H, 0:1], scale=1.0,
                              accum_out=scr[0:H, 1:2])
                s_.mark(sS)
            S_(s2)
            sm2 = sS.n

            def v2(v):
                v.wait_ge(sS.h, sm2)
                v.reciprocal(scr[0:H, 2:3], scr[0:H, 1:2])
                v.mark(sV)
            V_(v2)
            vrc = sV.n

            def s3(s_):
                s_.wait_ge(sV.h, vrc)
                s_.activation(qwb[0:H, 0:T], qsb[0:H, qscols], AF.Identity,
                              bias=0.0, scale=scr[0:H, 2:3])
                s_.mark(sS)
            S_(s3)
            qw_done = sS.n

            vmarks = {}
            tmarks = {}
            for c in range(DC):
                def t2(t_, c=c):
                    if c == 0:
                        t_.wait_ge(sS.h, qw_done)
                    ps = pA if c % 2 == 0 else pB
                    if c >= 2:
                        t_.wait_ge(sV.h, vmarks[c - 2])
                    for tt in range(4):
                        sl = slice(tt * 512, (tt + 1) * 512)
                        t_.matmul(ps[:, sl], sel16[0:H, c * P:(c + 1) * P],
                                  qwb[0:H, sl], start=True, stop=True)
                    t_.mark(sT)
                    tmarks[c] = sT.n
                T_(t2)

                def v3(v, c=c):
                    v.wait_ge(sT.h, tmarks[c])
                    ps = pA if c % 2 == 0 else pB
                    v.tensor_tensor(scr[:, 0:T], ps[:, 0:T], src_bf[:, c, :], ALU.mult)
                    v.tensor_reduce(pqk[:, outcol0 + c:outcol0 + c + 1], scr[:, 0:T],
                                    AX.X, ALU.add)
                    v.mark(sV)
                V_(v3)
                vmarks[c] = sV.n
            return sV.n

        if DBG == "mk":
            return tapb(mkb, sS.h, mk_done)
        if DBG == "qs0":
            mk_ = score_pool(mqb, wqa, bqa2[0:H, 0:1], slice(0, T), 0, sS.h, mk_done,
                             dbg_qs0=True)
            return tapb(qsb, sS.h, mk_, rows=H)
        pq_done = score_pool(mqb, wqa, bqa2[0:H, 0:1], slice(0, T), 0, sS.h, mk_done)
        if DBG == "qw":
            return tapb(qwb, sV.h, pq_done, rows=H)
        if DBG == "qse":
            return tapb(qsb, sV.h, pq_done, rows=H)
        if DBG == "pq":
            def vt2(v):
                v.wait_ge(sV.h, pq_done)
                for c in range(DC):
                    v.memset(x[:, c, :], 0.0)
                v.tensor_copy(x[:, 0, 0:2 * DC], pqk[:, :])
                v.mark(sV)
            V_(vt2)
            return sV.n

        def v_mqk(v):
            v.wait_ge(sV.h, pq_done)
            v.wait_ge(sS.h, mk_done)
            v.engine_nop(); v.engine_nop()
            for c in range(DC):
                v.tensor_scalar(mkb[:, c, :], mkb[:, c, :], pqk[:, c:c + 1], None, ALU.mult)
            v.mark(sV)
        V_(v_mqk)
        mqk_done = sV.n

        if DBG == "mqk":
            return tapb(mkb, sV.h, mqk_done)
        pk_done = score_pool(mkb, wka, bqa2[0:H, 1:2], slice(0, T), DC, sV.h, mqk_done)
        if DBG == "pk":
            def vt3(v):
                v.wait_ge(sV.h, pk_done)
                for c in range(DC):
                    v.memset(x[:, c, :], 0.0)
                v.tensor_copy(x[:, 0, 0:2 * DC], pqk[:, :])
                v.mark(sV)
            V_(vt3)
            return sV.n

        wv = xln

        def v_wv(v):
            v.wait_ge(sV.h, pk_done)
            v.engine_nop(); v.engine_nop()
            for c in range(DC):
                v.tensor_scalar(wv[:, c, :], mqb[:, c, :], pqk[:, DC + c:DC + c + 1],
                                None, ALU.mult)
            v.mark(sV)
        V_(v_wv)
        wv_done = sV.n

        def wt_epil(oc, ps, t_mark_fn):
            def s(s_, oc=oc, ps=ps):
                s_.wait_ge(sT.h, t_mark_fn())
                s_.activation(scr[:, 0:T], ps[:, :], AF.Identity,
                              bias=bt[:, oc:oc + 1], scale=1.0)
                s_.mark(sS)
            S_(s)
            sm = sS.n

            def v(v_, oc=oc, sm=sm):
                v_.wait_ge(sS.h, sm)
                v_.tensor_tensor(x[:, oc, :], x[:, oc, :], scr[:, 0:T], ALU.add)
                v_.tensor_tensor(x[:, oc, :], x[:, oc, :], mqb[:, oc, :], ALU.add)
                v_.mark(sV)
            V_(v)
            return (sV.h, sV.n)

        proj_d2d(wt, wv, wsl, 2, wt_epil, (sV.h, wv_done))
        return sV.n

    def phase_l0_ffn(prev_done):
        def spill(sy):
            sy.wait_ge(sV.h, prev_done)
            dS.inc(sy.dma_start(out=xspill.rearrange("(c p) t -> p c t", p=P), in_=x[:, :, :]), 16)
        SY(spill)
        spill_done = dS.n
        xsp = xspill.rearrange("(c p) t -> p c t", p=P)

        wb_marks = []
        for hh in range(2):
            wbq = {}
            t0 = hh * (T // 2)
            tn = T // 2
            AR.seek(0)
            R1 = AR.alloc([P, DC, tn], F32)
            AR.seek(32 * 1024)
            xln = AR.alloc([P, DC, tn], BF16)
            AR.seek(32 * 1024)
            hb = AR.alloc([P, FC, tn], BF16)
            AR.seek(96 * 1024)
            xbfs = AR.alloc([P, DC, tn], BF16)
            AR.seek(96 * 1024)
            u = AR.alloc([P, FC, tn], BF16)
            AR.seek(160 * 1024)
            wsl = [AR.alloc([P, FC, P], BF16) for _ in range(2)]
            w3b = [AR.alloc([P, D], BF16) for _ in range(2)]
            SCR0 = AR.off
            rows2 = AR.alloc([P, 2, tn], F32)
            AR.seek(SCR0)
            t0sc = AR.alloc([P, tn], BF16)
            AR.seek(SCR0)
            xout = AR.alloc([P, DC, 256], F32)
            g2sc = AR.alloc([P, 2, 256], BF16)
            assert AR.off <= CONST_OFF, AR.off

            def ld(sy, t0=t0, tn=tn, hh=hh):
                sy.wait_ge(dS.h, spill_done)
                if hh == 1:
                    sy.wait_ge(sV.h, half1_done)
                dX.inc(sy.dma_start(out=R1[:, :, :], in_=xsp[:, :, t0:t0 + tn]), 16)
            SY(ld)
            r1_done = dX.n

            def vwait(v, r1_done=r1_done):
                v.wait_ge(dX.h, r1_done)
                v.mark(sV)
            V_(vwait)
            r1v = sV.n

            ln_done = emit_ln(t0, tn, xln, xbfs, rows2[:, 0, :], rows2[:, 1, :],
                              xsrc=R1, extra_wait=(sV.h, r1v))

            w1src = w1t.rearrange("(c p) m -> p c m", p=P)
            w2src = w2t.rearrange("(c p) m -> p c m", p=P)
            wmark0 = dW.n
            mm12 = {}
            u_marks = {}
            for j in range(2 * FC):
                oc, wi = j // 2, j % 2
                b = j % 2

                def g(g_, j=j, oc=oc, wi=wi, b=b, ln_done=ln_done, wmark0=wmark0):
                    if j == 0:
                        g_.wait_ge(sV.h, ln_done)
                    if j >= 2:
                        g_.wait_ge(sT.h, mm12[j - 2])
                    src_ = w1src if wi == 0 else w2src
                    dW.inc(g_.dma_start(out=wsl[b][:, 0:DC, :],
                                        in_=src_[:, :, oc * P:(oc + 1) * P]), 16)
                G_(g)

                def t(t_, j=j, oc=oc, wi=wi, b=b, ln_done=ln_done, wmark0=wmark0):
                    if j == 0:
                        t_.wait_ge(sV.h, ln_done)
                    t_.wait_ge(dW.h, wmark0 + 16 * (j + 1))
                    ps = pA if wi == 0 else pB
                    psl = ps[:, (oc % 2) * 1024:(oc % 2 + 1) * 1024]
                    if oc >= 2 and wi == 0:
                        h_, n_ = u_marks[oc - 2]
                        t_.wait_ge(h_, n_)
                    for kc in range(DC):
                        for sub in range(2):
                            sl = slice(sub * 512, (sub + 1) * 512)
                            t_.matmul(psl[:, sl], wsl[b][:, kc, :], xln[:, kc, sl],
                                      start=(kc == 0), stop=(kc == DC - 1))
                    t_.mark(sT)
                    mm12[j] = sT.n
                T_(t)
                if wi == 1:
                    def s(s_, oc=oc):
                        s_.wait_ge(sT.h, mm12[2 * oc])
                        psl = pA[:, (oc % 2) * 1024:(oc % 2 + 1) * 1024]
                        s_.activation(t0sc[:, :], psl, AF.Silu, bias=0.0,
                                      scale=s1v[:, oc:oc + 1])
                        s_.mark(sS)
                    S_(s)
                    sm = sS.n

                    def v(v_, oc=oc, sm=sm):
                        v_.wait_ge(sS.h, sm)
                        v_.wait_ge(sT.h, mm12[2 * oc + 1])
                        psl = pB[:, (oc % 2) * 1024:(oc % 2 + 1) * 1024]
                        v_.scalar_tensor_tensor(u[:, oc, :], psl, s2v[:, oc:oc + 1],
                                                t0sc[:, :], ALU.mult, ALU.mult)
                        v_.mark(sV)
                    V_(v)
                    u_marks[oc] = (sV.h, sV.n)
            u_done = sV.n

            h1src = h1t.rearrange("(c p) m -> p c m", p=P)
            wm1 = dW.n
            mmh1 = {}
            geluk = {}
            for oc in range(FC):
                b = oc % 2

                def g(g_, oc=oc, b=b, u_done=u_done):
                    if oc == 0:
                        g_.wait_ge(sV.h, u_done)
                    if oc >= 2:
                        g_.wait_ge(sT.h, mmh1[oc - 2])
                    dW.inc(g_.dma_start(out=wsl[b][:, :, :],
                                        in_=h1src[:, :, oc * P:(oc + 1) * P]), 16)
                G_(g)

                def t(t_, oc=oc, b=b, u_done=u_done, wm1=wm1):
                    if oc == 0:
                        t_.wait_ge(sV.h, u_done)
                    t_.wait_ge(dW.h, wm1 + 16 * (oc + 1))
                    psl = (pA if oc % 2 == 0 else pB)[:, 0:1024]
                    if oc >= 2:
                        t_.wait_ge(sS.h, geluk[oc - 2])
                    for kc in range(FC):
                        for sub in range(2):
                            sl = slice(sub * 512, (sub + 1) * 512)
                            t_.matmul(psl[:, sl], wsl[b][:, kc, :], u[:, kc, sl],
                                      start=(kc == 0), stop=(kc == FC - 1))
                    t_.mark(sT)
                    mmh1[oc] = sT.n
                T_(t)

                def s(s_, oc=oc):
                    s_.wait_ge(sT.h, mmh1[oc])
                    psl = (pA if oc % 2 == 0 else pB)[:, 0:1024]
                    s_.activation(hb[:, oc, :], psl, AF.Gelu, bias=0.0,
                                  scale=sh1v[:, oc:oc + 1])
                    s_.mark(sS)
                S_(s)
                geluk[oc] = sS.n
            hb_done = sS.n

            h2src = h2t.rearrange("(c p) m -> p c m", p=P)
            wm2 = dW.n
            mmh2 = {}
            g2k = {}
            w3k = {}
            fink = {}
            for tq in range(4):
                tsl = slice(tq * 256, (tq + 1) * 256)
                for oc2 in range(FC):
                    j = tq * FC + oc2
                    b = oc2 % 2

                    def g(g_, j=j, oc2=oc2, b=b, hb_done=hb_done):
                        if j == 0:
                            g_.wait_ge(sS.h, hb_done)
                        if j >= 2:
                            g_.wait_ge(sT.h, w3k[j - 2])
                        dW.inc(g_.dma_start(out=wsl[b][:, :, :],
                                            in_=h2src[:, :, oc2 * P:(oc2 + 1) * P]), 16)
                        dW.inc(g_.dma_start(out=w3b[b][:, :],
                                            in_=w3t[oc2 * P:(oc2 + 1) * P, :]), 16)
                    G_(g)

                    def t(t_, j=j, tq=tq, oc2=oc2, b=b, tsl=tsl, hb_done=hb_done, wm2=wm2):
                        if j == 0:
                            t_.wait_ge(sS.h, hb_done)
                        t_.wait_ge(dW.h, wm2 + 32 * (j + 1))
                        psl = pA[:, 1024 + (j % 2) * 256:1024 + (j % 2) * 256 + 256]
                        if j >= 2:
                            t_.wait_ge(sS.h, g2k[j - 2])
                        if oc2 == 0 and tq >= 1:
                            h_, n_ = fink[tq - 1]
                            t_.wait_ge(h_, n_)
                        for kc in range(FC):
                            t_.matmul(psl, wsl[b][:, kc, :], hb[:, kc, tsl],
                                      start=(kc == 0), stop=(kc == FC - 1))
                        t_.mark(sT)
                        mmh2[j] = sT.n
                    T_(t)

                    def s(s_, j=j, oc2=oc2):
                        s_.wait_ge(sT.h, mmh2[j])
                        psl = pA[:, 1024 + (j % 2) * 256:1024 + (j % 2) * 256 + 256]
                        s_.activation(g2sc[:, j % 2, :], psl, AF.Gelu, bias=0.0,
                                      scale=sh2v[:, oc2:oc2 + 1])
                        s_.mark(sS)
                    S_(s)
                    g2k[j] = sS.n

                    def t2(t_, j=j, oc2=oc2, b=b):
                        t_.wait_ge(sS.h, g2k[j])
                        for o3 in range(DC):
                            t_.matmul(pB[:, o3 * 256:(o3 + 1) * 256],
                                      w3b[b][:, o3 * P:(o3 + 1) * P], g2sc[:, j % 2, :],
                                      start=(oc2 == 0), stop=(oc2 == FC - 1))
                        t_.mark(sT)
                        w3k[j] = sT.n
                    T_(t2)

                def vf(v, tq=tq):
                    v.wait_ge(sT.h, w3k[tq * FC + FC - 1])
                    if tq >= 1:
                        v.wait_ge(dS.h, wbq[tq - 1])
                    tsl_ = slice(tq * 256, (tq + 1) * 256)
                    for o3 in range(DC):
                        v.scalar_tensor_tensor(xout[:, o3, :],
                                               pB[:, o3 * 256:(o3 + 1) * 256],
                                               s3v[:, o3:o3 + 1], R1[:, o3, tsl_],
                                               ALU.mult, ALU.add)
                    v.mark(sV)
                V_(vf)
                fink[tq] = (sV.h, sV.n)

                def wb(sy, tq=tq, t0=t0):
                    sy.wait_ge(sV.h, fink[tq][1])
                    dS.inc(sy.dma_start(
                        out=xsp[:, :, t0 + tq * 256:t0 + (tq + 1) * 256],
                        in_=xout[:, :, :]), 16)
                SY(wb)
                wbq[tq] = dS.n if True else 0
            wb_marks.append(dS.n)
            def vh(v):
                v.wait_ge(dS.h, dS.n)
                v.mark(sV)
            wbn = dS.n

            def vh2(v, wbn=wbn):
                v.wait_ge(dS.h, wbn)
                v.mark(sV)
            V_(vh2)
            half1_done = sV.n

        def rl(sy):
            sy.wait_ge(dS.h, wb_marks[1])
            dX.inc(sy.dma_start(out=x[:, :, :], in_=xsp[:, :, :]), 16)
        SY(rl)
        rln = dX.n

        def vr(v):
            v.wait_ge(dX.h, rln)
            v.mark(sV)
        V_(vr)
        return sV.n

    def phase_l1_attn(prev_done):
        AR.seek(64 * 1024)
        xln = AR.alloc([P, DC, T], BF16)
        xbfs = AR.alloc([P, DC, T], BF16)
        alpha = AR.alloc([P, T], F32)
        tsc = AR.alloc([P, T], F32)
        t2sc = AR.alloc([P, T], F32)
        wsl = [AR.alloc([P, DC, P], BF16) for _ in range(4)]
        assert AR.off <= CONST_OFF, AR.off

        ln_done = emit_ln(0, T, xln, xbfs, alpha[:, 0:T], tsc[:, 0:T],
                          extra_wait=(sV.h, prev_done))

        wgsrc = wgt.rearrange("(c p) m -> p c m", p=P)
        wisrc = wint.rearrange("(c p) m -> p c m", p=P)
        wm = dW.n
        mmk = {}
        ck = {}
        for c in range(DC):
            for wi in range(2):
                j = 2 * c + wi
                b = j % 4

                def g(g_, j=j, wi=wi, c=c, b=b):
                    if j == 0:
                        g_.wait_ge(sV.h, ln_done)
                    if j >= 4:
                        g_.wait_ge(sT.h, mmk[j - 4])
                    src = wgsrc if wi == 0 else wisrc
                    dW.inc(g_.dma_start(out=wsl[b][:, :, :],
                                        in_=src[:, :, c * P:(c + 1) * P]), 16)
                G_(g)

                def t(t_, j=j, wi=wi, c=c, b=b):
                    if j == 0:
                        t_.wait_ge(sV.h, ln_done)
                    t_.wait_ge(dW.h, wm + 16 * (j + 1))
                    ps = pA if wi == 0 else pB
                    if c >= 1:
                        h_, n_ = ck[c - 1]
                        t_.wait_ge(h_, n_)
                    for kc in range(DC):
                        for tt in range(4):
                            sl = slice(tt * 512, (tt + 1) * 512)
                            t_.matmul(ps[:, sl], wsl[b][:, kc, :], xln[:, kc, sl],
                                      start=(kc == 0), stop=(kc == DC - 1))
                    t_.mark(sT)
                    mmk[j] = sT.n
                T_(t)

            def s1(s_, c=c):
                s_.wait_ge(sT.h, mmk[2 * c])
                s_.activation(tsc[:, :], pA[:, :], AF.Tanh,
                              bias=bgc[:, c:c + 1], scale=0.5)
                s_.nop()
                s_.activation(alpha[:, :], tsc[:, :], AF.Exp,
                              bias=clam[:, c:c + 1], scale=clam[:, c:c + 1])
                s_.nop()
                s_.wait_ge(sT.h, mmk[2 * c + 1])
                s_.activation(tsc[:, :], pB[:, :], AF.Tanh,
                              bias=binc[:, c:c + 1], scale=0.5)
                s_.mark(sS)
            S_(s1)
            sm1 = sS.n

            def v1(v, c=c, sm1=sm1):
                v.wait_ge(sS.h, sm1)
                v.tensor_tensor(t2sc[:, :], alpha[:, :], alpha[:, :], ALU.mult)
                v.mark(sV)
            V_(v1)
            vm1 = sV.n

            def s2(s_, c=c, vm1=vm1):
                s_.wait_ge(sV.h, vm1)
                s_.activation(t2sc[:, :], t2sc[:, :], AF.Sqrt,
                              bias=cpool[:, 2:3], scale=-1.0)
                s_.mark(sS)
            S_(s2)
            sm2 = sS.n

            def v2(v, c=c, sm2=sm2):
                v.wait_ge(sS.h, sm2)
                v.tensor_scalar(tsc[:, :], tsc[:, :], 0.5, 0.5, ALU.mult, ALU.add)
                v.nop()
                v.tensor_tensor(t2sc[:, :], t2sc[:, :], tsc[:, :], ALU.mult)
                v.nop()
                v.tensor_tensor(t2sc[:, :], t2sc[:, :], xln[:, c, :], ALU.mult)
                v.engine_nop(); v.engine_nop()
                v.tensor_tensor_scan(tsc[:, :], alpha[:, :], t2sc[:, :], 0.0,
                                     ALU.mult, ALU.add)
                v.engine_nop(); v.engine_nop()
                v.tensor_tensor(x[:, c, :], x[:, c, :], tsc[:, :], ALU.add)
                v.mark(sV)
            V_(v2)
            ck[c] = (sV.h, sV.n)
        return sV.n

    def phase_moe(prev_done):
        AR.seek(64 * 1024)
        xln = AR.alloc([P, DC, T], BF16)
        xbfs = AR.alloc([P, DC, T], BF16)
        rt = AR.alloc([P, NT, NE], F32)
        oh1 = AR.alloc([P, NT, NE], F32)
        oh2 = AR.alloc([P, NT, NE], F32)
        msc = AR.alloc([P, NT, NE], F32)
        mscb = AR.alloc([P, NT, NE], BF16)
        m1 = AR.alloc([P, NT], F32)
        m2 = AR.alloc([P, NT], F32)
        e1 = AR.alloc([P, NT], F32)
        e2 = AR.alloc([P, NT], F32)
        wv1 = AR.alloc([P, NT], F32)
        wv2 = AR.alloc([P, NT], F32)
        slt = [AR.alloc([P, NT], U32), AR.alloc([P, NT], U32)]
        sltf = [AR.alloc([P, NT], F32), AR.alloc([P, NT], F32)]
        agb = AR.alloc([P, 16], F32)
        SMALL_END = AR.off
        AR.seek(64 * 1024)
        ce = AR.alloc([P, T], F32)
        exl = AR.alloc([P, T], F32)
        exg = AR.alloc([P, T], F32)
        zt = AR.alloc([P, 4096], BF16)
        assert AR.off <= 96 * 1024, AR.off
        AR.push()
        AR.seek(SMALL_END)
        lnscr_a = AR.alloc([P, T], F32)
        lnscr_b = AR.alloc([P, T], F32)
        assert AR.off <= CONST_OFF, AR.off
        AR.pop()
        ln_done = emit_ln(0, T, xln, xbfs, lnscr_a, lnscr_b,
                          extra_wait=(sV.h, prev_done))
        xlnT = xbfs.rearrange("p c t -> p (c t)").rearrange("p (a b) -> p a b", a=NT)

        def spx(sy):
            sy.wait_ge(sV.h, ln_done)
            dS.inc(sy.dma_start(out=xspill.rearrange("(c p) t -> p c t", p=P),
                                in_=x[:, :, :]), 16)
        SY(spx)
        x_spilled = dS.n

        trk = {}
        evk = {}
        for grp in range(16):
            def t(t_, grp=grp):
                if grp == 0:
                    t_.wait_ge(sV.h, ln_done)
                ps = pAb if grp % 2 == 0 else pBb
                if grp >= 2:
                    t_.wait_ge(sV.h, evk[grp - 2])
                for c in range(DC):
                    t_.transpose(ps[:, c * P:(c + 1) * P],
                                 xln[:, c, grp * P:(grp + 1) * P], eyeb[:, :])
                t_.mark(sT)
                trk[grp] = sT.n
            T_(t)

            def v(v_, grp=grp):
                v_.wait_ge(sT.h, trk[grp])
                ps = pAb if grp % 2 == 0 else pBb
                v_.tensor_copy(xlnT[:, grp, :], ps[:, 0:D])
                v_.mark(sV)
            V_(v)
            evk[grp] = sV.n
        xlnT_done = sV.n

        def tg(t_):
            t_.wait_ge(sV.h, xlnT_done)
            t_.wait_ge(dW.h, 16)
            for tt4 in range(4):
                sl = slice(tt4 * 512, (tt4 + 1) * 512)
                for kc in range(DC):
                    t_.matmul(pB[:, sl], wgate_s[:, kc, :], xln[:, kc, sl],
                              start=(kc == 0), stop=(kc == DC - 1))
            t_.mark(sT)
        T_(tg)
        gmm = sT.n

        def vg(v):
            v.wait_ge(sT.h, gmm)
            v.tensor_copy(ce[0:NE, :], pB[0:NE, 0:T])
            v.mark(sV)
        V_(vg)
        scf = sV.n

        def ttr(t_):
            t_.wait_ge(sV.h, scf)
            for tt in range(NT):
                t_.transpose(pA[:, tt * NE:(tt + 1) * NE],
                             ce[0:NE, tt * P:(tt + 1) * P], eye8[0:8, :])
            t_.mark(sT)
        T_(ttr)
        trm = sT.n

        def vtr(v):
            v.wait_ge(sT.h, trm)
            v.tensor_copy(rt[:, :, :], pA[:, 0:NT * NE])
            v.mark(sV)
        V_(vtr)
        rt_done = sV.n

        def vt1(v):
            v.wait_ge(sV.h, rt_done)
            v.tensor_reduce(m1[:, :], rt[:, :, :], AX.X, ALU.max)
            v.engine_nop(); v.engine_nop()
            for sub in range(NT):
                v.tensor_scalar(oh1[:, sub, :], rt[:, sub, :], m1[:, sub:sub + 1],
                                None, ALU.is_equal)
            v.tensor_tensor(msc[:, :, :], oh1[:, :, :], iota8[:, :, :], ALU.mult)
            v.tensor_reduce(e1[:, :], msc[:, :, :], AX.X, ALU.add)
            v.scalar_tensor_tensor(msc[:, :, :], oh1[:, :, :], -1e9, rt[:, :, :],
                                   ALU.mult, ALU.add)
            v.tensor_reduce(m2[:, :], msc[:, :, :], AX.X, ALU.max)
            v.engine_nop(); v.engine_nop()
            for sub in range(NT):
                v.tensor_scalar(oh2[:, sub, :], msc[:, sub, :], m2[:, sub:sub + 1],
                                None, ALU.is_equal)
            v.tensor_tensor(msc[:, :, :], oh2[:, :, :], iota8[:, :, :], ALU.mult)
            v.tensor_reduce(e2[:, :], msc[:, :, :], AX.X, ALU.add)
            v.engine_nop()
            for sub in range(NT):
                v.tensor_scalar(msc[:, sub, :], rt[:, sub, :], m1[:, sub:sub + 1],
                                None, ALU.subtract)
            v.mark(sV)
        V_(vt1)
        vt1m = sV.n

        def st1(s_):
            s_.wait_ge(sV.h, vt1m)
            s_.activation(msc[:, :, :], msc[:, :, :], AF.Exp, bias=0.0, scale=1.0)
            s_.mark(sS)
        S_(st1)
        st1m = sS.n

        def vt2(v):
            v.wait_ge(sS.h, st1m)
            v.tensor_reduce(wv2[:, :], msc[:, :, :], AX.X, ALU.add)
            v.engine_nop(); v.engine_nop()
            v.reciprocal(wv1[:, :], wv2[:, :])
            v.engine_nop()
            v.tensor_tensor(m2[:, :], m2[:, :], m1[:, :], ALU.subtract)
            v.mark(sV)
        V_(vt2)
        vt2m = sV.n

        def st2(s_):
            s_.wait_ge(sV.h, vt2m)
            s_.activation(m2[:, :], m2[:, :], AF.Exp, bias=0.0, scale=1.0)
            s_.mark(sS)
        S_(st2)
        st2m = sS.n

        def vt3(v):
            v.wait_ge(sS.h, st2m)
            v.tensor_tensor(wv2[:, :], m2[:, :], wv1[:, :], ALU.mult)
            v.tensor_tensor(msc[:, :, :], oh1[:, :, :], oh2[:, :, :], ALU.add)
            v.nop()
            v.tensor_copy(mscb[:, :, :], msc[:, :, :])
            v.mark(sV)
        V_(vt3)
        vt3m = sV.n

        def ttr2(t_):
            t_.wait_ge(sV.h, vt3m)
            for tt in range(NT):
                t_.transpose(pAb[0:NE, tt * P:(tt + 1) * P], mscb[:, tt, :], eyeb[:, :])
            t_.mark(sT)
        T_(ttr2)
        tr2m = sT.n

        def vce(v):
            v.wait_ge(sT.h, tr2m)
            v.tensor_copy(ce[0:NE, :], pAb[0:NE, 0:T])
            v.engine_nop(); v.engine_nop()
            v.memset(exl[0:NE, :], 1.0)
            v.engine_nop(); v.engine_nop()
            v.tensor_tensor_scan(exg[0:NE, :], exl[0:NE, :], ce[0:NE, :], 0.0,
                                 ALU.mult, ALU.add)
            v.engine_nop(); v.engine_nop()
            v.tensor_tensor(exl[0:NE, :], exg[0:NE, :], ce[0:NE, :], ALU.subtract)
            v.mark(sV)
        V_(vce)
        exm = sV.n

        def syc(sy):
            sy.wait_ge(sV.h, exm)
            dS.inc(sy.dma_start(out=ccin.rearrange("(a b) -> a b", a=NE),
                                in_=exg[0:NE, T - 1:T]), 16)
        SY(syc)
        cntd = dS.n

        def gcc(g_):
            g_.wait_ge(dS.h, cntd)
            sC.inc(g_.collective_compute("AllGather", ALU.bypass,
                                         replica_groups=[list(range(8))],
                                         ins=[ccin], outs=[ccout]))
            g_.wait_ge(sC.h, sC.n)
            dX.inc(g_.dma_start(out=agb[0:8, 0:8],
                                in_=ccout.rearrange("(a b) -> a b", a=8)), 16)
        G_(gcc)
        agd = dX.n

        def tgo(t_):
            t_.wait_ge(dX.h, agd)
            t_.transpose(pA[0:8, 0:8], agb[0:8, 0:8], eye8[0:8, :])
            t_.mark(sT)
        T_(tgo)
        gom = sT.n

        def vgo(v):
            v.wait_ge(sT.h, gom)
            v.tensor_tensor(agb[0:8, 8:16], pA[0:8, 0:8], cm8[0:8, :], ALU.mult)
            v.nop()
            v.tensor_reduce(agb[0:8, 0:1], agb[0:8, 8:16], AX.X, ALU.add)
            v.engine_nop(); v.engine_nop()
            v.tensor_scalar(exg[0:NE, :], exl[0:NE, :], agb[0:NE, 0:1], None, ALU.add)
            v.mark(sV)
        V_(vgo)
        exgm = sV.n

        def ttr3(t_):
            t_.wait_ge(sV.h, exgm)
            for tt in range(NT):
                t_.transpose(pA[:, tt * NE:(tt + 1) * NE],
                             exl[0:NE, tt * P:(tt + 1) * P], eye8[0:8, :])
            for tt in range(NT):
                t_.transpose(pB[:, tt * NE:(tt + 1) * NE],
                             exg[0:NE, tt * P:(tt + 1) * P], eye8[0:8, :])
            t_.mark(sT)
        T_(ttr3)
        tr3m = sT.n

        def vpos(v):
            v.wait_ge(sT.h, tr3m)
            exlT = pA[:, 0:NT * NE].rearrange("p (a b) -> p a b", a=NT)
            exgT = pB[:, 0:NT * NE].rearrange("p (a b) -> p a b", a=NT)
            for k, oh, ee, wv_ in ((0, oh1, e1, wv1), (1, oh2, e2, wv2)):
                v.tensor_tensor(msc[:, :, :], oh[:, :, :], exlT, ALU.mult)
                v.tensor_reduce(m1[:, :], msc[:, :, :], AX.X, ALU.add)
                v.tensor_tensor(msc[:, :, :], oh[:, :, :], exgT, ALU.mult)
                v.tensor_reduce(m2[:, :], msc[:, :, :], AX.X, ALU.add)
                v.engine_nop()
                v.tensor_scalar(m2[:, :], m2[:, :], 4096.0, None, ALU.is_lt)
                v.engine_nop()
                v.tensor_tensor(wv_[:, :], wv_[:, :], m2[:, :], ALU.mult)
                v.tensor_scalar(m1[:, :], m1[:, :], float(CAP - 1), None, ALU.min)
                v.engine_nop()
                v.scalar_tensor_tensor(sltf[k][:, :], ee[:, :], float(CAP), m1[:, :],
                                       ALU.mult, ALU.add)
                v.engine_nop()
            v.mark(sV)
        V_(vpos)
        slotf_done = sV.n

        def gcast(g_):
            g_.wait_ge(sV.h, slotf_done)
            g_.tensor_copy(slt[0][:, :], sltf[0][:, :])
            g_.tensor_copy(slt[1][:, :], sltf[1][:, :])
            g_.mark(sG)
        G_(gcast)
        slotm = sG.n

        if os.environ.get("MK_DBG", "") == "moer":
            def vtap(v):
                v.wait_ge(sG.h, slotm)
                for c in range(DC):
                    v.memset(x[:, c, :], 0.0)
                v.tensor_copy(x[:, 0, 0:NT * NE], rt[:, :, :])
                v.tensor_copy(x[:, 1, 0:NT], e1[:, :])
                v.tensor_copy(x[:, 1, 16:32], e2[:, :])
                v.tensor_copy(x[:, 1, 32:48], wv1[:, :])
                v.tensor_copy(x[:, 1, 48:64], wv2[:, :])
                v.tensor_copy(x[:, 1, 64:80], sltf[0][:, :])
                v.tensor_copy(x[:, 1, 80:96], sltf[1][:, :])
                v.tensor_copy(x[:, 1, 96:112], e1[:, :])
                v.tensor_copy(x[:, 1, 112:128], e2[:, :])
                v.tensor_copy(x[:, 2, 0:128], oh1[:, :, :])
                v.tensor_copy(x[:, 3, 0:128], oh2[:, :, :])
                v.tensor_copy(x[:, 2, 128:144], m1[:, :])
                v.tensor_copy(x[:, 2, 144:160], m2[:, :])
                v.tensor_copy(x[0:8, 4, :], ce[0:8, :])
                v.tensor_copy(x[0:8, 5, :], exl[0:8, :])
                v.tensor_copy(x[0:8, 6, :], exg[0:8, :])
                v.tensor_copy(x[:, 1, 96:112], oh1[:, 0, :].bitcast(F32)[:, 0:8]) if False else None
                v.mark(sV)
            V_(vtap)
            return sV.n


        def vz(v):
            v.memset(zt[:, :], 0.0)
            v.mark(sV)
        V_(vz)
        zm = sV.n

        def syz(sy):
            sy.wait_ge(sV.h, zm)
            eiv = exp_in.rearrange("(n p) d -> n p d", p=P)
            for i in range(NSLOT * D // (P * 4096)):
                dS.inc(sy.dma_start(out=eiv[i * 4:(i + 1) * 4, :, :].transpose([1, 0, 2]),
                                    in_=zt[:, :]), 16)
        SY(syz)
        zd = dS.n

        def gsc(g_):
            g_.wait_ge(sG.h, slotm)
            g_.wait_ge(dS.h, zd)
            for k in range(2):
                for tt in range(NT):
                    dI.inc(g_.indirect_dma_start(
                        out=exp_in,
                        out_offset=bass.IndirectOffsetOnAxis(ap=slt[k][:, tt:tt + 1], axis=0),
                        in_=xlnT[:, tt, :], in_offset=None), 16)
        G_(gsc)
        scatd = dI.n

        AR.seek(0)
        abuf = AR.alloc([P, 12, 1024], BF16)
        hgbuf = AR.alloc([P, 12, 1024], BF16)
        cfb = [AR.alloc([P, DC, P], BF16) for _ in range(2)]
        silu1 = AR.alloc([P, 1024], BF16)
        assert AR.off <= 64 * 1024, AR.off
        AR.seek(64 * 1024)
        grpF = AR.alloc([P, DC, 1024], BF16)
        assert AR.off <= 96 * 1024, AR.off
        AR.seek(96 * 1024)
        grpT = AR.alloc([P, 8, D], BF16)
        outT = AR.alloc([P, 8, D], BF16)
        assert AR.off <= 128 * 1024, AR.off
        AR.seek(SMALL_END)
        cpb = AR.alloc([P, 12, D], BF16)
        assert AR.off <= CONST_OFF, AR.off

        NPASS = NE * 2
        tr_marks = {}
        trEv = {}
        cfm = {}
        hgm = {}
        cpm = {}
        cpev = {}
        grp_load = {}
        out_done = {}
        for ip in range(NPASS):
            e, half = ip // 2, ip % 2
            base_row = e * CAP + half * 1024

            def syg(sy, ip=ip, base_row=base_row):
                if ip == 0:
                    sy.wait_ge(dI.h, scatd)
                    sy.wait_ge(dS.h, x_spilled)
                else:
                    sy.wait_ge(sT.h, tr_marks[(ip - 1, 7)])
                dX.inc(sy.dma_start(
                    out=grpT[:, :, :],
                    in_=exp_in[base_row:base_row + 1024, :]
                    .rearrange("(a p) d -> p a d", p=P)), 16)
            SY(syg)
            grp_load[ip] = dX.n

            for g8 in range(8):
                def t(t_, ip=ip, g8=g8):
                    t_.wait_ge(dX.h, grp_load[ip])
                    if ip > 0 and g8 == 0:
                        h_, n_ = cpev[(ip - 1, 7)]
                        t_.wait_ge(h_, n_)
                    ps = pAb if g8 % 2 == 0 else pBb
                    if g8 >= 2:
                        t_.wait_ge(sV.h, trEv[(ip, g8 - 2)])
                    if ip > 0 and g8 < 2:
                        t_.wait_ge(sV.h, trEv[(ip - 1, 6 + g8)])
                    for c in range(DC):
                        t_.transpose(ps[:, c * P:(c + 1) * P],
                                     grpT[:, g8, c * P:(c + 1) * P], eyeb[:, :])
                    t_.mark(sT)
                    tr_marks[(ip, g8)] = sT.n
                T_(t)

                def v(v_, ip=ip, g8=g8):
                    v_.wait_ge(sT.h, tr_marks[(ip, g8)])
                    ps = pAb if g8 % 2 == 0 else pBb
                    for c in range(DC):
                        v_.tensor_copy(grpF[:, c, g8 * P:(g8 + 1) * P],
                                       ps[:, c * P:(c + 1) * P])
                    v_.mark(sV)
                V_(v)
                trEv[(ip, g8)] = sV.n
            grpF_done = sV.n

            cfsrc = cfc[e, :, :].rearrange("(c p) m -> p c m", p=P)
            cfw0 = dW.n
            for fc in range(24):
                b = fc % 2

                def g(g_, ip=ip, fc=fc, b=b, grpF_done=grpF_done, cfsrc=cfsrc, e=e):
                    if fc == 0:
                        g_.wait_ge(sV.h, grpF_done)
                    if fc >= 2:
                        g_.wait_ge(sT.h, cfm[(ip, fc - 2)])
                    dW.inc(g_.dma_start(out=cfb[b][:, :, :],
                                        in_=cfsrc[:, :, fc * P:(fc + 1) * P]), 16)
                    if fc == 23:
                        dW.inc(g_.dma_start(
                            out=cpb[:, :, :],
                            in_=cprj[e, :, :].rearrange("(c p) m -> p c m", p=P)), 16)
                G_(g)

                def t(t_, ip=ip, fc=fc, b=b, cfw0=cfw0, grpF_done=grpF_done):
                    if fc == 0:
                        t_.wait_ge(sV.h, grpF_done)
                    t_.wait_ge(dW.h, cfw0 + 16 * (fc + 1))
                    ps = pA if fc % 2 == 0 else pB
                    if fc >= 2:
                        h_, n_ = hgm[(ip, fc - 2)]
                        t_.wait_ge(h_, n_)
                    for kc in range(DC):
                        for sub in range(2):
                            t_.matmul(ps[:, sub * 512:(sub + 1) * 512],
                                      cfb[b][:, kc, :],
                                      grpF[:, kc, sub * 512:(sub + 1) * 512],
                                      start=(kc == 0), stop=(kc == DC - 1))
                    t_.mark(sT)
                    cfm[(ip, fc)] = sT.n
                T_(t)

                if fc < 12:
                    def v(v_, ip=ip, fc=fc):
                        v_.wait_ge(sT.h, cfm[(ip, fc)])
                        ps = pA if fc % 2 == 0 else pB
                        v_.tensor_copy(abuf[:, fc, :], ps[:, 0:1024])
                        v_.mark(sV)
                    V_(v)
                    hgm[(ip, fc)] = (sV.h, sV.n)
                else:
                    def s(s_, ip=ip, fc=fc):
                        s_.wait_ge(sT.h, cfm[(ip, fc)])
                        s_.activation(silu1[:, :], abuf[:, fc - 12, :], AF.Silu,
                                      bias=0.0, scale=1.0)
                        s_.mark(sS)
                    S_(s)
                    sim = sS.n

                    def v(v_, ip=ip, fc=fc, sim=sim):
                        v_.wait_ge(sS.h, sim)
                        ps = pA if fc % 2 == 0 else pB
                        v_.tensor_tensor(hgbuf[:, fc - 12, :], silu1[:, :],
                                         ps[:, 0:1024], ALU.mult)
                        v_.mark(sV)
                    V_(v)
                    hgm[(ip, fc)] = (sV.h, sV.n)
            hg_done = sV.n
            cp_loaded = dW.n

            for sc in range(8):
                def t(t_, ip=ip, sc=sc, hg_done=hg_done, cp_loaded=cp_loaded):
                    if sc == 0:
                        t_.wait_ge(sV.h, hg_done)
                        t_.wait_ge(dW.h, cp_loaded)
                    ps = pA if sc % 2 == 0 else pB
                    if sc >= 2:
                        h_, n_ = cpev[(ip, sc - 2)]
                        t_.wait_ge(h_, n_)
                    for fc2 in range(12):
                        for sub in range(2):
                            t_.matmul(ps[:, sub * 512:(sub + 1) * 512],
                                      hgbuf[:, fc2, sc * P:(sc + 1) * P],
                                      cpb[:, fc2, sub * 512:(sub + 1) * 512],
                                      start=(fc2 == 0), stop=(fc2 == 11))
                    t_.mark(sT)
                    cpm[(ip, sc)] = sT.n
                T_(t)

                def v(v_, ip=ip, sc=sc):
                    v_.wait_ge(sT.h, cpm[(ip, sc)])
                    if sc == 0 and out_done.get(ip - 1):
                        v_.wait_ge(dS.h, out_done[ip - 1])
                    ps = pA if sc % 2 == 0 else pB
                    v_.tensor_copy(outT[:, sc, :], ps[:, 0:D])
                    v_.mark(sV)
                V_(v)
                cpev[(ip, sc)] = (sV.h, sV.n)
            outTm = sV.n

            def syo(sy, ip=ip, base_row=base_row, outTm=outTm):
                sy.wait_ge(sV.h, outTm)
                dS.inc(sy.dma_start(
                    out=out_slots[base_row:base_row + 1024, :]
                    .rearrange("(a p) d -> p a d", p=P),
                    in_=outT[:, :, :]), 16)
            SY(syo)
            out_done[ip] = dS.n
        allout = dS.n

        AR.seek(64 * 1024)
        g0 = AR.alloc([P, 2, D], BF16)
        g1 = AR.alloc([P, 2, D], BF16)
        yb = AR.alloc([P, 2, D], BF16)
        yf = AR.alloc([P, D], F32)
        assert AR.off <= 96 * 1024, AR.off

        def rlx(sy):
            sy.wait_ge(dS.h, allout)
            dX.inc(sy.dma_start(out=x[:, :, :],
                                in_=xspill.rearrange("(c p) t -> p c t", p=P)), 16)
        SY(rlx)
        x_back = dX.n
        gm = {}
        cbm = {}
        trx = {}
        vxm = {}
        for tt in range(NT):
            b = tt % 2

            def g(g_, tt=tt, b=b):
                if tt == 0:
                    g_.wait_ge(dS.h, allout)
                if tt >= 2:
                    g_.wait_ge(sV.h, cbm[tt - 2])
                dI.inc(g_.indirect_dma_start(
                    out=g0[:, b, :], out_offset=None, in_=out_slots,
                    in_offset=bass.IndirectOffsetOnAxis(ap=slt[0][:, tt:tt + 1], axis=0)), 16)
                dI.inc(g_.indirect_dma_start(
                    out=g1[:, b, :], out_offset=None, in_=out_slots,
                    in_offset=bass.IndirectOffsetOnAxis(ap=slt[1][:, tt:tt + 1], axis=0)), 16)
                gm[tt] = dI.n
            G_(g)

            def v(v_, tt=tt, b=b):
                v_.wait_ge(dI.h, gm[tt])
                if tt >= 2:
                    v_.wait_ge(sT.h, trx[tt - 2])
                v_.tensor_scalar(yf[:, :], g0[:, b, :], wv1[:, tt:tt + 1],
                                 None, ALU.mult)
                v_.nop()
                v_.scalar_tensor_tensor(yb[:, b, :], g1[:, b, :], wv2[:, tt:tt + 1],
                                        yf[:, :], ALU.mult, ALU.add)
                v_.mark(sV)
            V_(v)
            cbm[tt] = sV.n

            def t(t_, tt=tt, b=b):
                t_.wait_ge(sV.h, cbm[tt])
                if tt >= 2:
                    v_h, v_n = vxm[tt - 2]
                    t_.wait_ge(v_h, v_n)
                ps = pAb if tt % 2 == 0 else pBb
                for c in range(DC):
                    t_.transpose(ps[:, c * P:(c + 1) * P],
                                 yb[:, b, c * P:(c + 1) * P], eyeb[:, :])
                t_.mark(sT)
                trx[tt] = sT.n
            T_(t)

            def vx(v_, tt=tt):
                v_.wait_ge(sT.h, trx[tt])
                v_.wait_ge(dX.h, x_back)
                ps = pAb if tt % 2 == 0 else pBb
                for c in range(DC):
                    v_.tensor_tensor(x[:, c, tt * P:(tt + 1) * P],
                                     x[:, c, tt * P:(tt + 1) * P],
                                     ps[:, c * P:(c + 1) * P], ALU.add)
                v_.mark(sV)
            V_(vx)
            vxm[tt] = (sV.h, sV.n)
        return sV.n

    def phase_ln_only():
        AR.seek(64 * 1024)
        xln = AR.alloc([P, DC, T], BF16)
        xbfs = AR.alloc([P, DC, T], BF16)
        scr = AR.alloc([P, T], F32)
        qsb = AR.alloc([P, T], F32)
        ln_done = emit_ln(0, T, xln, xbfs, scr[:, 0:T], qsb[:, 0:T])

        def vout(v):
            v.wait_ge(sV.h, ln_done)
            for c in range(DC):
                v.tensor_copy(x[:, c, :], xln[:, c, :])
            v.mark(sV)
        V_(vout)
        return sV.n

    prologue()
    done = sV.n
    if upto == 11:
        done = phase_ln_only()
    if 1 <= upto <= 9:
        done = phase_l0_attn()
    if 2 <= upto <= 9:
        done = phase_l0_ffn(done)
    if 3 <= upto <= 9:
        done = phase_l1_attn(done)
    if 4 <= upto <= 9:
        done = phase_moe(done)

    def fin(sy):
        sy.wait_ge(sV.h, sV.n)
        sy.wait_ge(sS.h, sS.n)
        sy.wait_ge(sT.h, sT.n)
        sy.wait_ge(sG.h, sG.n)
        sy.wait_ge(dI.h, dI.n)
        sy.wait_ge(dX.h, dX.n)
        sy.wait_ge(dW.h, dW.n)
        dS.inc(sy.dma_start(out=y_out.rearrange("(c p) t -> p c t", p=P), in_=x[:, :, :]), 16)
        sy.wait_ge(dS.h, dS.n)
    SY(fin)

    es.close()
    return nc


def _prep(inputs):
    f = np.float32
    ih = {}
    x = np.asarray(inputs["L0_attn_hidden_states"], f)
    ih["x_per_core"] = [np.ascontiguousarray(x[b].T) for b in range(8)]

    def tp(a):
        return np.ascontiguousarray(np.asarray(a, f).T)

    def cc(v, ncols):
        return np.ascontiguousarray(np.asarray(v, f).reshape(ncols, P).T)

    ih["wq"] = np.ascontiguousarray(np.asarray(inputs["L0_attn_Wq"], f))
    ih["wk"] = np.ascontiguousarray(np.asarray(inputs["L0_attn_Wk"], f))
    ih["wt"] = np.ascontiguousarray(np.asarray(inputs["L0_attn_Wt"], f))
    wqa_p = np.zeros((D, P), f); wqa_p[:, 0:H] = np.asarray(inputs["L0_attn_Wqa"], f)
    wka_p = np.zeros((D, P), f); wka_p[:, 0:H] = np.asarray(inputs["L0_attn_Wka"], f)
    ih["wqa"] = wqa_p
    ih["wka"] = wka_p
    ih["bq"] = cc(inputs["L0_attn_bq"], DC)
    ih["bk"] = cc(inputs["L0_attn_bk"], DC)
    ih["bt"] = cc(inputs["L0_attn_bt"], DC)
    ih["bqa"] = np.stack([np.asarray(inputs["L0_attn_bqa"], f) * 0.125,
                          np.asarray(inputs["L0_attn_bka"], f) * 0.125], axis=1)
    ih["w1t"] = tp(inputs["L0_ffn_w1_w"])
    ih["w2t"] = tp(inputs["L0_ffn_w2_w"])
    ih["h1t"] = tp(inputs["L0_ffn_h1_w"])
    ih["h2t"] = tp(inputs["L0_ffn_h2_w"])
    ih["w3t"] = tp(inputs["L0_ffn_w3_w"])
    ih["s1c"] = cc(inputs["L0_ffn_w1_s"], FC)
    ih["s2c"] = cc(inputs["L0_ffn_w2_s"], FC)
    ih["sh1c"] = cc(inputs["L0_ffn_h1_s"], FC)
    ih["sh2c"] = cc(inputs["L0_ffn_h2_s"], FC)
    ih["s3c"] = cc(inputs["L0_ffn_w3_s"], DC)
    ih["wint"] = tp(inputs["L1_attn_W_in"])
    ih["wgt"] = tp(inputs["L1_attn_W_gate"])
    ih["binc"] = cc(inputs["L1_attn_b_in"], DC) * 0.5
    ih["bgc"] = cc(inputs["L1_attn_b_gate"], DC) * 0.5
    ih["lamc"] = cc(inputs["L1_attn_forget_lambda"], DC)
    wg_p = np.zeros((D, P), f); wg_p[:, 0:NE] = np.asarray(inputs["L1_ffn_w_gate"], f)
    ih["wgate"] = wg_p
    ih["cfc"] = np.ascontiguousarray(np.asarray(inputs["L1_ffn_c_fc"], f))
    ih["cprj"] = np.ascontiguousarray(np.asarray(inputs["L1_ffn_c_proj"], f))
    ih["eyebf"] = np.eye(P, dtype=ml_dtypes.bfloat16)
    ih["eye8f"] = np.eye(8, dtype=f)
    sel = np.zeros((H, DC * P), ml_dtypes.bfloat16)
    for c in range(DC):
        for pp in range(P):
            sel[2 * c + pp // 64, c * P + pp] = 1.0
    ih["sel16"] = sel
    ih["onescol"] = np.ones((P, 1), ml_dtypes.bfloat16)
    ih["onesfull"] = np.ones((P, P), ml_dtypes.bfloat16)
    ih["ones1p"] = np.ones((1, P), f)
    ih["iota8"] = np.tile(np.arange(8, dtype=f), (P, NT))
    cp = np.zeros((P, 8), f)
    cp[:, 0] = 1.0
    cp[:, 1] = 1e-5
    cp[:, 2] = 1.000001
    ih["cpool"] = cp
    return ih


def kernel(**inputs):
    upto = int(os.environ.get("MK_UPTO", "4"))
    ih = _prep(inputs)
    nc = build_nc(upto)
    shared = {k: v for k, v in ih.items() if k != "x_per_core"}
    in_maps = []
    for b in range(8):
        m = dict(shared)
        m["x"] = ih["x_per_core"][b]
        mask = (np.arange(8) < b).astype(np.float32)
        m["cmask8"] = np.tile(mask, (8, 1))
        in_maps.append(m)
    res = run_bass_kernel_spmd(nc, in_maps, core_ids=list(range(8)))
    out = np.stack([np.ascontiguousarray(r["y"].T) for r in res.results], axis=0)
    return out
